# version 03 (final)
# baseline (speedup 1.0000x reference)
import math
import contextlib
import numpy as np

import concourse.bass as bass
import concourse.mybir as mybir
from concourse.bass_utils import run_bass_kernel_spmd

F32 = mybir.dt.float32
BF16 = mybir.dt.bfloat16
I32 = mybir.dt.int32
AF = mybir.ActivationFunctionType
ALU = mybir.AluOpType
AX = mybir.AxisListType
ET = mybir.EngineType

D = 2048
DK = 16
S = 4096
TOWN = 2048
TB = 512
NTT = 4
HD = 128
NH = 16
FD = 512
TNH = 4
HID = 5461
HT = 43
CH = 1024
NEG = -30000.0
SCL = 1.0 / math.sqrt(HD)
BASE_LR_INV = 0.001 + math.log(-math.expm1(-0.001))

PHASES = 99
P2PARTS = "abc"
P2A_LVL = 9
NOBRANCH = False
NBLK = 4
P7OUT = "final"
GDTILES = [0, 1, 2, 3]
YHC = 43


class Ctx:
    def __init__(self, nc, stack):
        self.nc = nc
        self.stack = stack
        self.sems = {}
        self.cnt = {}
        self.waited = {}
        self.engs = [nc.sync, nc.tensor, nc.vector, nc.scalar, nc.gpsimd]

    suffix = ""

    def sem(self, name):
        if name != "PH":
            name = name + self.suffix
        if name not in self.sems:
            self.sems[name] = self.stack.enter_context(self.nc.semaphore(name))
        if name not in self.cnt:
            self.cnt[name] = 0
        return self.sems[name]

    _last_auto = None

    def inc(self, instr, name, k=1):
        if self._last_auto == (id(instr), name):
            self._last_auto = None
            return instr
        h = self.sem(name)
        nm = h.name
        instr.then_inc(h, k)
        self.cnt[nm] += k
        return instr

    def wait(self, eng, name, v=None):
        h = self.sem(name)
        nm = h.name
        tgt = self.cnt.get(nm, 0) if v is None else v
        if tgt <= 0:
            return
        key = (eng.engine, nm)
        if self.waited.get(key, -1) >= tgt:
            return
        eng.wait_ge(h, tgt)
        self.waited[key] = tgt

    def _wait_raw(self, eng, nm, tgt):
        if tgt <= 0:
            return
        key = (eng.engine, nm)
        if self.waited.get(key, -1) >= tgt:
            return
        eng.wait_ge(self.sems[nm], tgt)
        self.waited[key] = tgt

    def barrier(self):
        for eng in self.engs:
            for nm in list(self.cnt):
                if nm.startswith("D"):
                    self._wait_raw(eng, nm, self.cnt[nm])
            self.inc(eng.nop(), "PH", 1)
        tgt = self.cnt["PH"]
        for eng in self.engs:
            eng.wait_ge(self.sem("PH"), tgt)
        self.waited = {(e.engine, nm): self.cnt[nm] for e in self.engs for nm in self.cnt}


class EngWrap:
    _PASS = frozenset(["wait_ge", "dma_start", "nop", "br", "engine",
                       "wait_ge_any", "dma_reset", "sem_clear"])

    def __init__(self, eng, name, C):
        self._e = eng
        self._n = name
        self._C = C

    def __getattr__(self, attr):
        real = getattr(self._e, attr)
        if attr.startswith("_") or attr in EngWrap._PASS or not callable(real):
            return real
        C = self._C

        def f(*a, **kw):
            C.wait(self._e, self._n)
            op = real(*a, **kw)
            C.inc(op, self._n, 1)
            C._last_auto = (id(op), self._n)
            return op
        return f


def build_program(debug_dump=()):
    nc = bass.Bass("TRN2", target_bir_lowering=False)
    stack = contextlib.ExitStack()
    C = Ctx(nc, stack)
    sy, te, gp = nc.sync, nc.tensor, nc.gpsimd
    ve = EngWrap(nc.vector, "VE", C)
    sc = EngWrap(nc.scalar, "SC", C)

    def din(n, s, d):
        return nc.dram_tensor(n, s, d, kind="ExternalInput").ap()

    x_h = din("x", [S, D], F32)
    wqk_h = din("wqk", [D, 4108], F32)
    wv_h = din("wv", [D, D], F32)
    wo_h = din("wo", [D, D], F32)
    w0t_h = din("w0t", [TNH, FD, FD], F32)
    w2t_h = din("w2t", [TNH, FD, FD], F32)
    w1n_h = din("w1n", [TNH, FD, FD], F32)
    vecs_h = din("vecs", [D, 8], F32)
    lrb_h = din("lrb", [128, 1], F32)
    ttw_h = din("ttw", [1, FD], F32)
    wf1_h = din("wf1", [D, HID], F32)
    wf3_h = din("wf3", [D, HID], F32)
    wf2_h = din("wf2", [HID, D], F32)
    ropec_h = din("ropec", [64, S], F32)
    ropes_h = din("ropes", [64, S], F32)
    masks_h = din("masks", [2, 128, 128], F32)
    ninv_h = din("ninv", [128, 16], F32)
    ident_h = din("ident", [128, 128], F32)
    meta_h = din("meta", [1, 8], I32)
    y_h = nc.dram_tensor("y", [TOWN, D], F32, kind="ExternalOutput").ap()

    def dsc(n, s, d):
        return nc.dram_tensor(n, s, d).ap()

    s_xlnF = dsc("s_xlnF", [D, S], BF16)
    s_aq = dsc("s_aq", [D, TOWN], BF16)
    s_fq = dsc("s_fq", [D, TOWN], BF16)
    s_ak = dsc("s_ak", [D, 3072], BF16)
    s_av = dsc("s_av", [3072, D], BF16)
    s_fkF = dsc("s_fkF", [D, S], BF16)
    s_fkT = dsc("s_fkT", [S, D], BF16)
    s_fvF = dsc("s_fvF", [D, S], BF16)
    s_fvT = dsc("s_fvT", [S, D], BF16)
    s_rq = dsc("s_rq", [S, 4], F32)
    s_rk = dsc("s_rk", [S, 4], F32)
    s_lr = dsc("s_lr", [S, 12], F32)
    s_attn = dsc("s_attn", [TOWN, D], BF16)
    s_mixF = dsc("s_mixF", [D, TOWN], BF16)
    s_xmid = dsc("s_xmid", [TOWN, D], F32)
    s_xmidF = dsc("s_xmidF", [D, TOWN], BF16)
    s_w1q = dsc("s_w1q", [D, HID], BF16)
    s_w3q = dsc("s_w3q", [D, HID], BF16)
    s_w2q = dsc("s_w2q", [HT * 128, D], BF16)
    scratch = {t.tensor.name: t for t in
               [s_xlnF, s_aq, s_fq, s_ak, s_av, s_fkF, s_fkT, s_fvF, s_fvT,
                s_rq, s_rk, s_lr, s_attn, s_mixF, s_xmid, s_xmidF, s_w1q, s_w3q, s_w2q]}
    dbg_out = {}
    for name in debug_dump:
        t = scratch[name]
        dbg_out[name] = nc.dram_tensor("dbg_" + name, list(t.tensor.shape),
                                       t.tensor.dtype, kind="ExternalOutput").ap()

    def sb(n, s, d, st=None):
        return (st or stack).enter_context(nc.sbuf_tensor(n, s, d))

    meta_sb = sb("meta_sb", [1, 8], I32)
    cst = sb("cst", [128, 4], F32)
    identb = sb("identb", [128, 128], BF16)
    identf = sb("identf", [128, 128], F32)
    vecs_sb = sb("vecs_sb", [128, DK, 8], F32)
    lrb_sb = sb("lrb_sb", [128, 1], F32)
    onescol_f = sb("onescol_f", [128, 1], F32)
    onesb = sb("onesb", [128, 1], BF16)
    onesrow = sb("onesrow", [1, 128], F32)
    ttwb = sb("ttwb", [128, FD], BF16)
    ttw_row = sb("ttw_row", [1, FD], F32)
    maskf = sb("maskf", [128, 128], F32)
    maskd = sb("maskd", [128, 128], F32)
    ninv_sb = sb("ninv_sb", [128, 16], F32)
    rq_all = sb("rq_all", [128, 32, 4], F32)
    rk_all = sb("rk_all", [128, 32, 4], F32)
    lr_all = sb("lr_all", [128, 32, 12], F32)
    s1col = sb("s1col", [128, 1], F32)
    s23col = sb("s23col", [128, 1], F32)

    pb = [stack.enter_context(nc.psum_tensor(f"pb{i}", [128, 512], F32)) for i in range(5)]
    pbt2 = [stack.enter_context(nc.psum_tensor(f"pbt{i}", [128, 1024], BF16)) for i in range(2)]
    pbs = stack.enter_context(nc.psum_tensor("pbs", [128, 512], F32))

    bank_guard = {}

    def gwait(eng, key):
        if not isinstance(key, tuple):
            key = id(key)
        g = bank_guard.get(key)
        if g is not None:
            C._wait_raw(eng, g[0], g[1])

    def gset(key, sem_name):
        if not isinstance(key, tuple):
            key = id(key)
        nm = C.sem(sem_name).name
        bank_guard[key] = (nm, C.cnt[nm])

    nc.const_aps.register(F32, 1.0, cst[:, 0:1])

    xlnF_r = s_xlnF.rearrange("(c p) s -> p c s", p=128)
    aq_r = s_aq.rearrange("(c p) s -> p c s", p=128)
    fq_r = s_fq.rearrange("(c p) s -> p c s", p=128)
    ak_r = s_ak.rearrange("(c p) s -> p c s", p=128)
    fkF_r = s_fkF.rearrange("(c p) s -> p c s", p=128)
    fvF_r = s_fvF.rearrange("(c p) s -> p c s", p=128)
    mixF_r = s_mixF.rearrange("(c p) s -> p c s", p=128)
    xmidF_r = s_xmidF.rearrange("(c p) s -> p c s", p=128)

    with nc.Block() as block:
        par_regs = nc.alloc_registers(
            "par", engines=[ET.SP, ET.PE, ET.DVE, ET.Activation, ET.Pool])

        def branch_pair(fn_even, fn_odd):
            pre = dict(C.cnt)
            prew = dict(C.waited)
            preg = dict(bank_guard)
            with nc.If_cmp(par_regs, 0, "IS_EQ"):
                fn_even()
            even_cnt = dict(C.cnt)
            C.cnt = dict(pre)
            C.waited = dict(prew)
            bank_guard.clear()
            bank_guard.update(preg)
            with nc.Else():
                fn_odd()
            assert C.cnt == even_cnt, ("branch sem mismatch",
                                       {k: (even_cnt.get(k), C.cnt.get(k))
                                        for k in set(even_cnt) | set(C.cnt)
                                        if even_cnt.get(k) != C.cnt.get(k)})
            C.waited = {}

        def p0_setup():
            for dst, src in [(meta_sb[:], meta_h[:, :]), (identf[:], ident_h[:, :]),
                             (vecs_sb[:], vecs_h.rearrange("(c p) k -> p c k", p=128)),
                             (lrb_sb[:], lrb_h[:, :]), (ttw_row[:], ttw_h[:, :]),
                             (maskf[:], masks_h[0]), (maskd[:], masks_h[1]),
                             (ninv_sb[:], ninv_h[:, :])]:
                C.inc(sy.dma_start(out=dst, in_=src), "Dc", 16)
            C.inc(gp.dma_start(out=identb[:], in_=ident_h[:, :]), "Dg", 16)
            C.wait(ve, "Dc")
            ve.memset(cst[:, 0:1], 1.0)
            ve.memset(onescol_f[:], 1.0)
            C.inc(ve.memset(onesrow[:], 1.0), "VE", 1)
            C.wait(gp, "Dg")
            C.inc(gp.memset(onesb[:], 1.0), "GP", 1)
            C.wait(te, "VE")
            C.wait(te, "Dc")
            C.inc(te.matmul(pbs[:, 0:FD], onesrow[:, :], ttw_row[:, :], start=True, stop=True), "TE", 1)
            C.wait(sc, "TE")
            C.inc(sc.copy(out=ttwb[:], in_=pbs[:, 0:FD]), "SC", 1)
            gset(pbs, "SC")
            for eng in C.engs:
                C.wait(eng, "Dc")
            nc.regs_load(par_regs, meta_sb[:1, 0:1])
            zt = sb("zt_p0", [128, 1024], BF16)
            C.inc(ve.memset(zt[:], 0.0), "VE", 1)
            C.wait(sy, "VE")
            for r in range(DK):
                C.inc(sy.dma_start(out=s_ak[r * 128:(r + 1) * 128, 0:1024], in_=zt[:, :]), "Dz", 16)
            for r in range(8):
                C.inc(sy.dma_start(out=s_av[r * 128:(r + 1) * 128, 0:1024], in_=zt[:, :]), "Dz", 16)
                C.inc(sy.dma_start(out=s_av[r * 128:(r + 1) * 128, 1024:2048], in_=zt[:, :]), "Dz", 16)
            C.barrier()

        def p1_layernorm():
            p1 = contextlib.ExitStack()
            xf = [sb(f"p1_xf{i}", [128, NTT, D], F32, p1) for i in range(2)]
            sm = sb("p1_sm", [128, NTT], F32, p1)
            sq = sb("p1_sq", [128, NTT], F32, p1)
            nmr = sb("p1_nmr", [128, NTT], F32, p1)
            sqs = sb("p1_sqs", [128, D], F32, p1)
            xlnT = sb("p1_xlnT", [128, NTT, D], BF16, p1)
            xlnF = sb("p1_xlnF", [128, DK, TB], BF16, p1)

            for b in range(8):
                buf = xf[b % 2]
                if b >= 2:
                    C.wait(sy, "P1a", (b - 1) * NTT)
                C.inc(sy.dma_start(out=buf[:], in_=x_h[b * TB:(b + 1) * TB, :]
                                   .rearrange("(t p) d -> p t d", p=128)), "Dx", 16)
                C.wait(ve, "Dx")
                for tt in range(NTT):
                    xin = buf[:, tt, :]
                    C.inc(ve.reduce_sum(out=sm[:, tt:tt + 1], in_=xin, axis=AX.X), "VE", 1)
                    C.inc(ve.scalar_tensor_tensor(out=sqs[:], in0=xin, scalar=1.0, in1=xin,
                                                  op0=ALU.mult, op1=ALU.mult,
                                                  accum_out=sq[:, tt:tt + 1]), "VE", 1)
                C.inc(ve.tensor_scalar_mul(out=sm[:], in0=sm[:], scalar1=1.0 / D), "VE", 1)
                C.inc(ve.tensor_scalar_mul(out=sq[:], in0=sq[:], scalar1=1.0 / D), "VE", 1)
                C.wait(ve, "VE")
                C.inc(ve.scalar_tensor_tensor(out=nmr[:], in0=sm[:], scalar=-1.0, in1=sm[:],
                                              op0=ALU.mult, op1=ALU.mult), "VE", 1)
                C.inc(ve.tensor_add(out=sq[:], in0=sq[:], in1=nmr[:]), "VE", 1)
                C.inc(ve.tensor_scalar_add(out=sq[:], in0=sq[:], scalar1=1e-5), "VE", 1)
                C.wait(ve, "VE")
                C.inc(ve.reciprocal(out=sq[:], in_=sq[:]), "VE", 1)
                C.wait(sc, "VE")
                C.inc(sc.sqrt(out=sq[:], in_=sq[:]), "SC", 1)
                C.wait(ve, "SC")
                C.inc(ve.scalar_tensor_tensor(out=nmr[:], in0=sm[:], scalar=-1.0, in1=sq[:],
                                              op0=ALU.mult, op1=ALU.mult), "VE", 1)
                C.wait(sc, "VE")
                for tt in range(NTT):
                    C.inc(sc.activation(out=xlnT[:, tt, :], in_=buf[:, tt, :], func=AF.Identity,
                                        bias=nmr[:, tt:tt + 1], scale=sq[:, tt:tt + 1]), "SC", 1)
                    C.inc(sc.nop(), "P1a", 1)
                C.wait(te, "SC")
                if b > 0:
                    C.wait(te, "Do", 16 * b)
                for dc in range(DK):
                    bk = pbt2[dc % 2]
                    gwait(te, bk)
                    for tt in range(NTT):
                        C.inc(te.transpose(bk[:, tt * 128:(tt + 1) * 128],
                                           xlnT[:, tt, dc * 128:(dc + 1) * 128], identb[:, :]), "TE", 1)
                    C.wait(sc, "TE")
                    C.inc(sc.copy(out=xlnF[:, dc, :], in_=bk[:, 0:512]), "SC", 1)
                    gset(bk, "SC")
                C.wait(sy, "SC")
                C.inc(sy.dma_start(out=xlnF_r[:, :, b * TB:(b + 1) * TB], in_=xlnF[:]), "Do", 16)
            C.barrier()
            p1.close()

        p2 = contextlib.ExitStack()

        def p2_alloc():
            t = {}
            t["wwt"] = sb("p2_wwt", [128, DK, 2060], BF16, p2)
            t["xq"] = [sb("p2_xq0", [128, DK, TB], BF16, p2)] * 2
            t["qf"] = sb("p2_qf", [128, DK, TB], BF16, p2)
            t["qsq"] = sb("p2_qsq", [128, TB], BF16, p2)
            t["ssqrow"] = sb("p2_ssqrow", [1, TB], F32, p2)
            t["rrow"] = sb("p2_rrow", [1, TB], F32, p2)
            t["rqb"] = sb("p2_rqb", [128, TB], F32, p2)
            t["crt"] = sb("p2_crt", [128, TB], BF16, p2)
            t["srt"] = sb("p2_srt", [128, TB], BF16, p2)
            t["rc"] = sb("p2_rc", [128, TB], F32, p2)
            t["rs"] = sb("p2_rs", [128, TB], F32, p2)
            t["qfw"] = sb("p2_qfw", [128, DK, TB], BF16, p2)
            t["qfs"] = sb("p2_qfs", [128, DK, TB], BF16, p2)
            t["aqblk"] = sb("p2_aqblk", [128, DK, TB], BF16, p2)
            t["fqblk"] = sb("p2_fqblk", [128, DK, TB], BF16, p2)
            t["f4"] = sb("p2_f4", [1, TNH * TB], F32, p2)
            t["r4"] = sb("p2_r4", [128, NTT, 4], F32, p2)
            t["lrs"] = sb("p2_lrs", [12, TB], F32, p2)
            t["lrt1"] = sb("p2_lrt1", [12, TB], F32, p2)
            t["lrt2"] = sb("p2_lrt2", [12, TB], F32, p2)
            t["fkT"] = None
            t["sqs2"] = sb("p2_sqs2", [128, FD], F32, p2)
            t["fkT"] = t["qfs"][:].rearrange("p a b -> p (a b)").rearrange(
                "p (a b) -> p a b", a=NTT)
            return t

        def rope_head(t, h, dst):
            a, b = t["qfw"][:, h, :], t["qfs"][:, h, :]
            C.inc(ve.tensor_mul(out=dst[:, :], in0=a[:, :], in1=t["crt"][:, :]), "VE", 1)
            C.inc(ve.tensor_mul(out=t["qsq"][:, :], in0=b[:, :], in1=t["srt"][:, :]), "VE", 1)
            C.inc(ve.tensor_sub(out=dst[0:64, :], in0=dst[0:64, :], in1=t["qsq"][0:64, :]), "VE", 1)
            C.inc(ve.tensor_add(out=dst[64:128, :], in0=dst[64:128, :], in1=t["qsq"][64:128, :]), "VE", 1)

        def p2_load_rope_rstd(t):
            C.wait(te, "SC")
            gwait(te, pbs)
            C.inc(te.matmul(pbs[0:128, 0:TB], onesrow[:, 0:128], t["rrow"][:, :],
                            start=True, stop=True), "TE", 1)
            C.wait(ve, "TE")
            C.inc(ve.tensor_copy(out=t["rqb"][:], in_=pbs[0:128, 0:TB]), "VE", 1)
            gset(pbs, "VE")
            C.wait(ve, "VE")
            C.wait(ve, "Ds")
            C.inc(ve.tensor_mul(out=t["crt"][:], in0=t["rc"][:], in1=t["rqb"][:]), "VE", 1)
            C.inc(ve.tensor_mul(out=t["srt"][:], in0=t["rs"][:], in1=t["rqb"][:]), "VE", 1)

        def p2a_block(t, b, lb):
            g0 = b * TB
            xbuf = t["xq"][b % 2]
            C.wait(ve, "Do")
            C.wait(sc, "Do")
            C.wait(sy, "TE")
            C.wait(sy, "VE")
            C.inc(sy.dma_start(out=xbuf[:], in_=xlnF_r[:, :, g0:g0 + TB]), "Ds", 16)
            C.inc(sy.dma_start(out=t["rc"][0:64, :], in_=ropec_h[:, g0:g0 + TB]), "Ds", 16)
            C.inc(sy.dma_start(out=t["rc"][64:128, :], in_=ropec_h[:, g0:g0 + TB]), "Ds", 16)
            C.inc(sy.dma_start(out=t["rs"][0:64, :], in_=ropes_h[:, g0:g0 + TB]), "Ds", 16)
            C.inc(sy.dma_start(out=t["rs"][64:128, :], in_=ropes_h[:, g0:g0 + TB]), "Ds", 16)
            C.wait(te, "Ds")
            C.wait(te, "Dw")
            if P2A_LVL < 1.2:
                return
            for mt in range(DK):
                bank = pb[mt % 2]
                gwait(te, bank)
                for kc in range(DK):
                    C.inc(te.matmul(bank[:, :], t["wwt"][:, kc, mt * 128:(mt + 1) * 128],
                                    xbuf[:, kc, :], start=(kc == 0), stop=(kc == DK - 1)), "TE", 1)
                if P2A_LVL < 1.4:
                    continue
                C.wait(sc, "TE")
                C.inc(sc.copy(out=t["qf"][:, mt, :], in_=bank[:, :]), "SC", 1)
                if P2A_LVL >= 1.6:
                    C.inc(sc.activation(out=t["qsq"][:, :], in_=bank[:, :], func=AF.Square), "SC", 1)
                gset(bank, "SC")
                if P2A_LVL < 1.8:
                    continue
                C.wait(te, "SC")
                if mt == 0:
                    gwait(te, pbs)
                C.inc(te.matmul(pbs[0:1, 0:TB], onesb[:, :], t["qsq"][:, :],
                                start=(mt == 0), stop=(mt == DK - 1)), "TE", 1)
            if P2A_LVL < 2:
                return
            C.wait(ve, "TE")
            C.inc(ve.tensor_scalar(out=t["ssqrow"][:], in0=pbs[0:1, 0:TB], scalar1=1.0 / D,
                                   scalar2=1e-6, op0=ALU.mult, op1=ALU.add), "VE", 1)
            gset(pbs, "VE")
            C.wait(ve, "VE")
            C.inc(ve.reciprocal(out=t["rrow"][:], in_=t["ssqrow"][:]), "VE", 1)
            C.wait(sc, "VE")
            C.inc(sc.sqrt(out=t["rrow"][:], in_=t["rrow"][:]), "SC", 1)
            p2_load_rope_rstd(t)
            if P2A_LVL < 3:
                return
            C.wait(ve, "SC")
            for ch in range(DK):
                C.inc(ve.tensor_scalar_mul(out=t["qfw"][:, ch, :], in0=t["qf"][:, ch, :],
                                           scalar1=vecs_sb[:, ch, 0:1]), "VE", 1)
            C.wait(sy, "VE")
            C.inc(sy.dma_start(out=t["qfs"][0:64, :, :], in_=t["qfw"][64:128, :, :]), "Ds2", 16)
            C.inc(sy.dma_start(out=t["qfs"][64:128, :, :], in_=t["qfw"][0:64, :, :]), "Ds2", 16)
            C.wait(ve, "Ds2")
            for h in range(NH):
                rope_head(t, h, t["aqblk"][:, h, :])
            if P2A_LVL < 4:
                return
            for ch in range(DK):
                C.wait(sc, "Ds")
                C.inc(sc.activation(out=t["fqblk"][:, ch, :], in_=t["qf"][:, ch, :], func=AF.Silu), "SC", 1)
                C.wait(ve, "SC")
                C.inc(ve.tensor_scalar(out=t["fqblk"][:, ch, :], in0=t["fqblk"][:, ch, :],
                                       scalar1=vecs_sb[:, ch, 2:3], scalar2=vecs_sb[:, ch, 3:4],
                                       op0=ALU.mult, op1=ALU.add), "VE", 1)
            if P2A_LVL < 5:
                return
            for ht in range(TNH):
                for cc in range(4):
                    ch = ht * 4 + cc
                    C.wait(sc, "VE")
                    C.wait(sc, "TE")
                    C.inc(sc.activation(out=t["qsq"][:, :], in_=t["fqblk"][:, ch, :], func=AF.Square), "SC", 1)
                    C.wait(te, "SC")
                    if ht == 0 and cc == 0:
                        gwait(te, pbs)
                    C.inc(te.matmul(pbs[0:1, 0:TB], onesb[:, :], t["qsq"][:, :],
                                    start=(cc == 0), stop=(cc == 3)), "TE", 1)
                C.wait(sc, "TE")
                C.inc(sc.sqrt(out=t["f4"][0:1, ht * TB:(ht + 1) * TB], in_=pbs[0:1, 0:TB]), "SC", 1)
                gset(pbs, "SC")
            C.wait(ve, "SC")
            C.inc(ve.tensor_scalar_add(out=t["f4"][:, :], in0=t["f4"][:, :], scalar1=1e-12), "VE", 1)
            C.wait(ve, "VE")
            C.inc(ve.reciprocal(out=t["f4"][:, :], in_=t["f4"][:, :]), "VE", 1)
            C.wait(te, "VE")
            gwait(te, pb[2])
            for tt in range(NTT):
                for ht in range(TNH):
                    C.inc(te.transpose(pb[2][0:128, (tt * 4 + ht):(tt * 4 + ht) + 1],
                                       t["f4"][0:1, ht * TB + tt * 128: ht * TB + (tt + 1) * 128],
                                       identf[0:1, 0:1]), "TE", 1)
            C.wait(ve, "TE")
            gt0 = g0 // 128
            for tt in range(NTT):
                C.inc(ve.tensor_copy(out=rq_all[:, gt0 + tt, :],
                                     in_=pb[2][0:128, tt * 4:(tt + 1) * 4]), "VE", 1)
            gset(pb[2], "VE")
            if P2A_LVL < 6:
                return
            C.wait(sy, "VE")
            C.wait(sy, "SC")
            C.inc(sy.dma_start(out=aq_r[:, :, lb:lb + TB], in_=t["aqblk"][:]), "Do", 16)
            C.inc(sy.dma_start(out=fq_r[:, :, lb:lb + TB], in_=t["fqblk"][:]), "Do", 16)

        def p2b_block(t, b, store_ak):
            g0 = b * TB
            gt0 = g0 // 128
            xbuf = t["xq"][b % 2]
            C.wait(ve, "Do")
            C.wait(sc, "Do")
            C.wait(sy, "TE")
            C.wait(sy, "VE")
            C.inc(sy.dma_start(out=xbuf[:], in_=xlnF_r[:, :, g0:g0 + TB]), "Ds", 16)
            C.inc(sy.dma_start(out=t["rc"][0:64, :], in_=ropec_h[:, g0:g0 + TB]), "Ds", 16)
            C.inc(sy.dma_start(out=t["rc"][64:128, :], in_=ropec_h[:, g0:g0 + TB]), "Ds", 16)
            C.inc(sy.dma_start(out=t["rs"][0:64, :], in_=ropes_h[:, g0:g0 + TB]), "Ds", 16)
            C.inc(sy.dma_start(out=t["rs"][64:128, :], in_=ropes_h[:, g0:g0 + TB]), "Ds", 16)
            C.wait(te, "Ds")
            C.wait(te, "Dw")
            for mt in range(DK + 1):
                bank = pb[mt % 2]
                rows = 128 if mt < DK else 12
                gwait(te, bank)
                for kc in range(DK):
                    C.inc(te.matmul(bank[0:rows, :], t["wwt"][:, kc, mt * 128:mt * 128 + rows],
                                    xbuf[:, kc, :], start=(kc == 0), stop=(kc == DK - 1)), "TE", 1)
                C.wait(sc, "TE")
                if mt < DK:
                    C.inc(sc.copy(out=t["qf"][:, mt, :], in_=bank[:, :]), "SC", 1)
                    C.inc(sc.activation(out=t["qsq"][:, :], in_=bank[:, :], func=AF.Square), "SC", 1)
                    gset(bank, "SC")
                    C.wait(te, "SC")
                    if mt == 0:
                        gwait(te, pbs)
                    C.inc(te.matmul(pbs[0:1, 0:TB], onesb[:, :], t["qsq"][:, :],
                                    start=(mt == 0), stop=(mt == DK - 1)), "TE", 1)
                else:
                    C.inc(sc.copy(out=t["lrs"][:, :], in_=bank[0:12, :]), "SC", 1)
                    gset(bank, "SC")
            C.wait(ve, "TE")
            C.inc(ve.tensor_scalar(out=t["ssqrow"][:], in0=pbs[0:1, 0:TB], scalar1=1.0 / D,
                                   scalar2=1e-6, op0=ALU.mult, op1=ALU.add), "VE", 1)
            gset(pbs, "VE")
            C.wait(ve, "VE")
            C.inc(ve.reciprocal(out=t["rrow"][:], in_=t["ssqrow"][:]), "VE", 1)
            C.wait(sc, "VE")
            C.inc(sc.sqrt(out=t["rrow"][:], in_=t["rrow"][:]), "SC", 1)
            p2_load_rope_rstd(t)
            C.wait(ve, "SC")
            C.inc(ve.tensor_scalar(out=t["lrs"][:], in0=t["lrs"][:], scalar1=lrb_sb[0:12, 0:1],
                                   scalar2=BASE_LR_INV, op0=ALU.add, op1=ALU.add), "VE", 1)
            C.wait(ve, "VE")
            C.inc(ve.tensor_scalar_max(out=t["lrt1"][:], in0=t["lrs"][:], scalar1=0.0), "VE", 1)
            C.wait(sc, "VE")
            C.inc(sc.activation(out=t["lrt2"][:], in_=t["lrs"][:], func=AF.Abs), "SC", 1)
            C.inc(sc.activation(out=t["lrt2"][:], in_=t["lrt2"][:], func=AF.Exp, scale=-1.0), "SC", 1)
            C.inc(sc.activation(out=t["lrt2"][:], in_=t["lrt2"][:], func=AF.Ln, bias=1.0), "SC", 1)
            C.wait(ve, "SC")
            C.inc(ve.tensor_add(out=t["lrs"][:], in0=t["lrt1"][:], in1=t["lrt2"][:]), "VE", 1)
            C.wait(te, "VE")
            gwait(te, pb[2])
            for tt in range(NTT):
                C.inc(te.transpose(pb[2][0:128, tt * 16:tt * 16 + 12],
                                   t["lrs"][:, tt * 128:(tt + 1) * 128], identf[0:12, 0:12]), "TE", 1)
            C.wait(ve, "TE")
            for tt in range(NTT):
                C.inc(ve.tensor_copy(out=lr_all[:, gt0 + tt, :],
                                     in_=pb[2][0:128, tt * 16:tt * 16 + 12]), "VE", 1)
            gset(pb[2], "VE")
            C.wait(ve, "SC")
            for ch in range(DK):
                C.inc(ve.tensor_scalar_mul(out=t["qfw"][:, ch, :], in0=t["qf"][:, ch, :],
                                           scalar1=vecs_sb[:, ch, 1:2]), "VE", 1)
            C.wait(sy, "VE")
            C.inc(sy.dma_start(out=t["qfs"][0:64, :, :], in_=t["qfw"][64:128, :, :]), "Ds2", 16)
            C.inc(sy.dma_start(out=t["qfs"][64:128, :, :], in_=t["qfw"][0:64, :, :]), "Ds2", 16)
            C.wait(ve, "Ds2")
            for h in range(NH):
                rope_head(t, h, t["aqblk"][:, h, :])
            for ch in range(DK):
                C.inc(sc.activation(out=t["fqblk"][:, ch, :], in_=t["qf"][:, ch, :], func=AF.Silu), "SC", 1)
                C.wait(ve, "SC")
                C.inc(ve.tensor_scalar(out=t["fqblk"][:, ch, :], in0=t["fqblk"][:, ch, :],
                                       scalar1=vecs_sb[:, ch, 4:5], scalar2=vecs_sb[:, ch, 5:6],
                                       op0=ALU.mult, op1=ALU.add), "VE", 1)
            C.wait(te, "VE")
            for tt in range(NTT):
                for grp in range(4):
                    bk = pbt2[grp % 2]
                    gwait(te, bk)
                    for i in range(4):
                        dc = grp * 4 + i
                        C.inc(te.transpose(bk[:, i * 128:(i + 1) * 128],
                                           t["fqblk"][:, dc, tt * 128:(tt + 1) * 128], identb[:, :]), "TE", 1)
                    C.wait(sc, "TE")
                    C.inc(sc.copy(out=t["fkT"][:, tt, grp * 512:(grp + 1) * 512],
                                  in_=bk[:, 0:512]), "SC", 1)
                    gset(bk, "SC")
            for tt in range(NTT):
                C.wait(ve, "SC")
                for ht in range(TNH):
                    C.inc(ve.scalar_tensor_tensor(out=t["sqs2"][:], in0=t["fkT"][:, tt, ht * FD:(ht + 1) * FD],
                                                  scalar=1.0, in1=t["fkT"][:, tt, ht * FD:(ht + 1) * FD],
                                                  op0=ALU.mult, op1=ALU.mult,
                                                  accum_out=t["r4"][:, tt, ht:ht + 1]), "VE", 1)
            C.wait(sc, "VE")
            C.inc(sc.sqrt(out=t["r4"][:], in_=t["r4"][:]), "SC", 1)
            C.wait(ve, "SC")
            C.inc(ve.tensor_scalar_add(out=t["r4"][:], in0=t["r4"][:], scalar1=1e-12), "VE", 1)
            C.wait(ve, "VE")
            C.inc(ve.reciprocal(out=t["r4"][:], in_=t["r4"][:]), "VE", 1)
            C.wait(ve, "VE")
            for tt in range(NTT):
                C.inc(ve.tensor_copy(out=rk_all[:, gt0 + tt, :], in_=t["r4"][:, tt, :]), "VE", 1)
                for ht in range(TNH):
                    C.inc(ve.tensor_scalar_mul(out=t["fkT"][:, tt, ht * FD:(ht + 1) * FD],
                                               in0=t["fkT"][:, tt, ht * FD:(ht + 1) * FD],
                                               scalar1=t["r4"][:, tt, ht:ht + 1]), "VE", 1)
            C.wait(sy, "VE")
            C.wait(sy, "SC")
            C.inc(sy.dma_start(out=fkF_r[:, :, g0:g0 + TB], in_=t["fqblk"][:]), "Do", 16)
            C.inc(sy.dma_start(out=s_fkT[g0:g0 + TB, :].rearrange("(t p) d -> p t d", p=128),
                               in_=t["fkT"][:]), "Do", 16)
            if store_ak is not None:
                C.inc(sy.dma_start(out=ak_r[:, :, store_ak:store_ak + TB], in_=t["aqblk"][:]), "Do", 16)
            else:
                C.inc(sy.nop(), "Do", 16)

        def p2c_block(t, b, store_av):
            g0 = b * TB
            xbuf = t["xq"][b % 2]
            C.wait(ve, "Do")
            C.wait(sc, "Do")
            C.wait(sy, "TE")
            C.wait(sy, "SC")
            C.inc(sy.dma_start(out=xbuf[:], in_=xlnF_r[:, :, g0:g0 + TB]), "Ds", 16)
            C.wait(te, "Ds")
            C.wait(te, "Dw")
            for tt in range(NTT):
                for nf in range(4):
                    bank = pb[(tt * 4 + nf) % 2]
                    gwait(te, bank)
                    for kc in range(DK):
                        C.inc(te.matmul(bank[:, :], xbuf[:, kc, tt * 128:(tt + 1) * 128],
                                        t["wwt"][:, kc, nf * 512:(nf + 1) * 512],
                                        start=(kc == 0), stop=(kc == DK - 1)), "TE", 1)
                    C.wait(sc, "TE")
                    C.inc(sc.copy(out=t["vblk"][:, tt, nf * 512:(nf + 1) * 512], in_=bank[:, :]), "SC", 1)
                    C.inc(sc.activation(out=t["fvblk"][:, tt, nf * 512:(nf + 1) * 512],
                                        in_=bank[:, :], func=AF.Silu), "SC", 1)
                    gset(bank, "SC")
            C.wait(te, "SC")
            for dc in range(DK):
                bk = pbt2[dc % 2]
                gwait(te, bk)
                for tt in range(NTT):
                    C.inc(te.transpose(bk[:, tt * 128:(tt + 1) * 128],
                                       t["fvblk"][:, tt, dc * 128:(dc + 1) * 128], identb[:, :]), "TE", 1)
                C.wait(sc, "TE")
                C.inc(sc.copy(out=t["fvF"][:, dc, :], in_=bk[:, 0:512]), "SC", 1)
                gset(bk, "SC")
            C.wait(sy, "SC")
            C.inc(sy.dma_start(out=s_fvT[g0:g0 + TB, :].rearrange("(t p) d -> p t d", p=128),
                               in_=t["fvblk"][:]), "Do", 16)
            C.inc(sy.dma_start(out=fvF_r[:, :, g0:g0 + TB], in_=t["fvF"][:]), "Do", 16)
            if store_av is not None:
                C.inc(sy.dma_start(out=s_av[store_av:store_av + TB, :]
                                   .rearrange("(t p) d -> p t d", p=128), in_=t["vblk"][:]), "Do", 16)
            else:
                C.inc(sy.nop(), "Do", 16)

        def p2_all(parts="abc"):
            t = p2_alloc()
            C.inc(gp.dma_start(out=t["wwt"][:, :, 0:2048], in_=wqk_h[:, 0:2048]
                               .rearrange("(c p) m -> p c m", p=128)), "Dw", 16)
            if "a" in parts:
                if P2A_LVL >= 1:
                    if NOBRANCH:
                        for b in range(4):
                            p2a_block(t, b, b * TB)
                    else:
                        branch_pair(
                            lambda: [p2a_block(t, b, b * TB) for b in range(NBLK)],
                            lambda: [p2a_block(t, b, (b - 4) * TB) for b in range(4, 4 + NBLK)])
            C.barrier()
            if "b" in parts:
                C.inc(gp.dma_start(out=t["wwt"][:, :, :], in_=wqk_h[:, 2048:4108]
                                   .rearrange("(c p) m -> p c m", p=128)), "Dw", 16)
                for b in range(8):
                    branch_pair(
                        lambda b=b: p2b_block(t, b, b * TB + 1024 if b <= 3 else None),
                        lambda b=b: p2b_block(t, b, b * TB - 1024 if b >= 2 else None))
                C.barrier()
            if "c" in parts:
                t["vblk"] = t["qf"][:].rearrange("p a b -> p (a b)").rearrange("p (a b) -> p a b", a=NTT)
                t["fvblk"] = t["aqblk"][:].rearrange("p a b -> p (a b)").rearrange("p (a b) -> p a b", a=NTT)
                t["fvF"] = t["fqblk"]
                C.inc(gp.dma_start(out=t["wwt"][:, :, 0:2048], in_=wv_h[:, :]
                                   .rearrange("(c p) m -> p c m", p=128)), "Dw", 16)
                for b in range(8):
                    branch_pair(
                        lambda b=b: p2c_block(t, b, b * TB + 1024 if b <= 3 else None),
                        lambda b=b: p2c_block(t, b, b * TB - 1024 if b >= 2 else None))
                C.barrier()
            p2.close()


        def p3_quant():
            p3 = contextlib.ExitStack()
            wt = [sb(f"p3_wt{i}", [128, HID], F32, p3) for i in range(2)]
            a_bf = sb("p3_a", [128, HID], BF16, p3)
            b_bf = sb("p3_b", [128, HID], BF16, p3)
            t_bf = sb("p3_t", [128, HID], BF16, p3)
            csum = sb("p3_csum", [128, 2], F32, p3)
            scal = sb("p3_scal", [1, 2], F32, p3)
            thr = sb("p3_thr", [128, 2], F32, p3)

            for w_h, s_out, ntile, tcols, numel, is_w1 in [
                    (wf1_h, s_w1q, 16, HID, D * HID, True),
                    (wf3_h, s_w3q, 16, HID, D * HID, False),
                    (wf2_h, s_w2q, 43, D, HID * D, False)]:
                C.inc(ve.memset(csum[:, 0:2], 0.0), "VE", 1)
                for i in range(ntile):
                    rows = min(128, w_h.shape[0] - i * 128)
                    buf = wt[i % 2]
                    C.wait(sy, "VE")
                    C.inc(sy.dma_start(out=buf[0:rows, 0:tcols],
                                       in_=w_h[i * 128:i * 128 + rows, :]), "Dq", 16)
                    C.wait(ve, "Dq")
                    C.wait(ve, "VE")
                    if rows < 128:
                        C.inc(ve.memset(csum[:, 1:2], 0.0), "VE", 1)
                        C.wait(ve, "VE")
                    C.inc(ve.tensor_reduce(out=csum[0:rows, 1:2], in_=buf[0:rows, 0:tcols],
                                           axis=AX.X, op=ALU.add, apply_absolute_value=True), "VE", 1)
                    C.wait(ve, "VE")
                    C.inc(ve.tensor_add(out=csum[:, 0:1], in0=csum[:, 0:1], in1=csum[:, 1:2]), "VE", 1)
                C.wait(te, "VE")
                gwait(te, pbs)
                C.inc(te.matmul(pbs[0:1, 0:1], csum[:, 0:1], onescol_f[:, :],
                                start=True, stop=True), "TE", 1)
                C.wait(ve, "TE")
                C.inc(ve.tensor_scalar_mul(out=scal[0:1, 0:1], in0=pbs[0:1, 0:1],
                                           scalar1=1.0 / numel), "VE", 1)
                gset(pbs, "VE")
                C.wait(te, "VE")
                gwait(te, pbs)
                C.inc(te.matmul(pbs[0:128, 0:1], onesrow[:, 0:128], scal[0:1, 0:1],
                                start=True, stop=True), "TE", 1)
                C.wait(ve, "TE")
                if is_w1:
                    C.inc(ve.tensor_copy(out=s1col[:, :], in_=pbs[0:128, 0:1]), "VE", 1)
                elif s_out is s_w3q:
                    C.inc(ve.tensor_copy(out=s23col[:, :], in_=pbs[0:128, 0:1]), "VE", 1)
                else:
                    C.inc(ve.tensor_mul(out=s23col[:, :], in0=s23col[:, :],
                                        in1=pbs[0:128, 0:1]), "VE", 1)
                C.inc(ve.tensor_scalar_mul(out=thr[:, 0:1], in0=pbs[0:128, 0:1], scalar1=0.5), "VE", 1)
                C.inc(ve.tensor_scalar_mul(out=thr[:, 1:2], in0=pbs[0:128, 0:1], scalar1=-0.5), "VE", 1)
                gset(pbs, "VE")
                for i in range(ntile):
                    rows = min(128, w_h.shape[0] - i * 128)
                    buf = wt[i % 2]
                    C.wait(sy, "GP")
                    C.wait(sy, "VE")
                    C.inc(sy.dma_start(out=buf[0:rows, 0:tcols],
                                       in_=w_h[i * 128:i * 128 + rows, :]), "Dq", 16)
                    C.wait(ve, "Dq")
                    C.wait(gp, "Dq")
                    C.wait(ve, "VE")
                    C.wait(ve, "Do")
                    C.inc(ve.tensor_single_scalar(out=a_bf[0:rows, 0:tcols], in_=buf[0:rows, 0:tcols],
                                                  scalar=thr[0:rows, 0:1], op=ALU.is_ge), "VE", 1)
                    C.wait(gp, "VE")
                    C.inc(gp.tensor_single_scalar(out=b_bf[0:rows, 0:tcols], in_=buf[0:rows, 0:tcols],
                                                  scalar=thr[0:rows, 1:2], op=ALU.is_le), "GP", 1)
                    C.wait(ve, "GP")
                    if rows < 128:
                        C.inc(ve.memset(t_bf[64:128, 0:tcols], 0.0), "VE", 1)
                    C.inc(ve.tensor_sub(out=t_bf[0:rows, 0:tcols], in0=a_bf[0:rows, 0:tcols],
                                        in1=b_bf[0:rows, 0:tcols]), "VE", 1)
                    srows = 128 if rows < 128 else rows
                    C.wait(sy, "VE")
                    C.inc(sy.dma_start(out=s_out[i * 128:i * 128 + srows, :],
                                       in_=t_bf[0:srows, 0:tcols]), "Do", 16)
            C.barrier()
            p3.close()

        def p4_attention():
            p4 = contextlib.ExitStack()
            aqg = sb("p4_aq", [128, 4, TOWN], BF16, p4)
            akg = sb("p4_ak", [128, 4, 3072], BF16, p4)
            avg = sb("p4_av", [128, 24, 512], BF16, p4)
            p_sb = sb("p4_p", [128, 1152], BF16, p4)
            pT_sb = sb("p4_pT", [128, 9, 128], BF16, p4)
            cols = sb("p4_cols", [128, 16], F32, p4)
            attn4 = sb("p4_attn", [128, 2, 512], BF16, p4)

            for hg in range(4):
                C.wait(sy, "TE")
                C.wait(sy, "SC")
                C.wait(sy, "VE")
                for h4 in range(4):
                    r0 = hg * 512 + h4 * 128
                    C.inc(sy.dma_start(out=aqg[:, h4, :], in_=s_aq[r0:r0 + 128, :]), "Ds", 16)
                    C.inc(sy.dma_start(out=akg[:, h4, :], in_=s_ak[r0:r0 + 128, :]), "Ds", 16)
                C.inc(sy.dma_start(out=avg[:], in_=s_av[:, hg * 512:(hg + 1) * 512]
                                   .rearrange("(t p) c -> p t c", p=128)), "Ds", 16)
                C.wait(te, "Ds")
                for j in range(16):
                    for h4 in range(4):
                        for t3 in range(3):
                            bank = pb[t3]
                            gwait(te, bank)
                            C.inc(te.matmul(bank[:, 0:384], aqg[:, h4, j * 128:(j + 1) * 128],
                                            akg[:, h4, j * 128 + t3 * 384: j * 128 + (t3 + 1) * 384],
                                            start=True, stop=True), "TE", 1)
                        C.wait(ve, "TE")
                        C.wait(ve, "Do")
                        C.inc(ve.tensor_add(out=pb[0][:, 0:128], in0=pb[0][:, 0:128],
                                            in1=maskf[:, :]), "VE", 1)
                        C.inc(ve.tensor_add(out=pb[2][:, 256:384], in0=pb[2][:, 256:384],
                                            in1=maskd[:, :]), "VE", 1)
                        C.wait(ve, "VE")
                        for t3 in range(3):
                            C.inc(ve.reduce_max(out=cols[:, t3:t3 + 1], in_=pb[t3][:, 0:384],
                                                axis=AX.X), "VE", 1)
                        C.wait(ve, "VE")
                        C.inc(ve.tensor_max(out=cols[:, 3:4], in0=cols[:, 0:1], in1=cols[:, 1:2]), "VE", 1)
                        C.wait(ve, "VE")
                        C.inc(ve.tensor_max(out=cols[:, 3:4], in0=cols[:, 3:4], in1=cols[:, 2:3]), "VE", 1)
                        C.wait(ve, "VE")
                        C.inc(ve.tensor_scalar_mul(out=cols[:, 4:5], in0=cols[:, 3:4],
                                                   scalar1=-SCL), "VE", 1)
                        C.wait(sc, "VE")
                        C.wait(sc, "TE")
                        for t3 in range(3):
                            C.inc(sc.activation(out=p_sb[:, t3 * 384:(t3 + 1) * 384],
                                                in_=pb[t3][:, 0:384], func=AF.Exp,
                                                bias=cols[:, 4:5], scale=SCL,
                                                accum_out=cols[:, 6 + t3:7 + t3]), "SC", 1)
                            gset(pb[t3], "SC")
                        C.inc(sc.activation(out=cols[:, 5:6], in_=cols[:, 4:5], func=AF.Exp), "SC", 1)
                        C.wait(ve, "SC")
                        C.inc(ve.tensor_add(out=cols[:, 9:10], in0=cols[:, 6:7], in1=cols[:, 7:8]), "VE", 1)
                        C.wait(ve, "VE")
                        C.inc(ve.tensor_add(out=cols[:, 9:10], in0=cols[:, 9:10], in1=cols[:, 8:9]), "VE", 1)
                        C.inc(ve.tensor_scalar_mul(out=cols[:, 10:11], in0=cols[:, 5:6],
                                                   scalar1=ninv_sb[:, j:j + 1]), "VE", 1)
                        C.wait(ve, "VE")
                        C.inc(ve.tensor_sub(out=cols[:, 9:10], in0=cols[:, 9:10], in1=cols[:, 10:11]), "VE", 1)
                        C.wait(ve, "VE")
                        C.inc(ve.reciprocal(out=cols[:, 11:12], in_=cols[:, 9:10]), "VE", 1)
                        C.wait(te, "SC")
                        for grp in range(3):
                            bk = pbt2[grp % 2]
                            gwait(te, bk)
                            for i in range(3):
                                tk = grp * 3 + i
                                C.inc(te.transpose(bk[:, i * 128:(i + 1) * 128],
                                                   p_sb[:, tk * 128:(tk + 1) * 128], identb[:, :]), "TE", 1)
                            C.wait(sc, "TE")
                            C.inc(sc.copy(out=pT_sb[:, grp * 3:(grp + 1) * 3, :],
                                          in_=bk[:, 0:384]), "SC", 1)
                            gset(bk, "SC")
                        C.wait(te, "SC")
                        gwait(te, pb[3])
                        for tk in range(9):
                            C.inc(te.matmul(pb[3][:, 0:128], pT_sb[:, tk, :],
                                            avg[:, j + tk, h4 * 128:(h4 + 1) * 128],
                                            start=(tk == 0), stop=(tk == 8)), "TE", 1)
                        C.wait(ve, "TE")
                        C.inc(ve.tensor_scalar_mul(out=attn4[:, j % 2, h4 * 128:(h4 + 1) * 128],
                                                   in0=pb[3][:, 0:128],
                                                   scalar1=cols[:, 11:12]), "VE", 1)
                        gset(pb[3], "VE")
                    C.wait(sy, "VE")
                    C.inc(sy.dma_start(out=s_attn[j * 128:(j + 1) * 128, hg * 512:(hg + 1) * 512],
                                       in_=attn4[:, j % 2, :]), "Do", 16)
            C.barrier()
            p4.close()

        def p5_ttt():
            p5 = contextlib.ExitStack()
            w0t_s = sb("p5_w0t", [128, 4, FD], F32, p5)
            w2t_s = sb("p5_w2t", [128, 4, FD], F32, p5)
            w1n_s = sb("p5_w1n", [128, 4, FD], F32, p5)
            w0b = sb("p5_w0b", [128, 4, FD], BF16, p5)
            w2b = sb("p5_w2b", [128, 4, FD], BF16, p5)
            w1b = sb("p5_w1b", [128, 4, FD], BF16, p5)
            w1t = sb("p5_w1t", [128, 4, FD], BF16, p5)
            kF = sb("p5_kF", [128, 4, CH], BF16, p5)
            vF = sb("p5_vF", [128, 4, CH], BF16, p5)
            kT = sb("p5_kT", [128, 8, FD], BF16, p5)
            vT = sb("p5_vT", [128, 8, FD], BF16, p5)
            qF = sb("p5_qF", [128, 4, CH], BF16, p5)
            at = sb("p5_at", [128, 8, FD], BF16, p5)
            hid = sb("p5_hid", [128, 8, FD], BF16, p5)
            dgl0 = sb("p5_dgl0", [128, 8, FD], BF16, p5)
            dhl2 = sb("p5_dhl2", [128, 8, FD], BF16, p5)
            fvl1 = sb("p5_fvl1", [128, 8, FD], BF16, p5)
            g32 = sb("p5_g32", [128, FD], F32, p5)
            h32 = sb("p5_h32", [128, FD], F32, p5)
            d32 = sb("p5_d32", [128, FD], F32, p5)
            sg = sb("p5_sg", [128, FD], F32, p5)
            sig = sb("p5_sig", [128, FD], F32, p5)
            tf = sb("p5_tf", [128, FD], F32, p5)
            t2 = sb("p5_t2", [128, FD], F32, p5)
            dgp = sb("p5_dgp", [128, FD], F32, p5)
            hq_bf = sb("p5_hq", [128, FD], BF16, p5)
            hqT = sb("p5_hqT", [128, 4, CH], BF16, p5)
            mixF = sb("p5_mixF", [128, 4, CH], BF16, p5)
            oisq = sb("p5_oisq", [128, FD], BF16, p5)
            oi32 = sb("p5_oi32", [128, FD], F32, p5)
            mixT = sb("p5_mixT", [128, FD], BF16, p5)
            ocol = sb("p5_ocol", [128, 4], F32, p5)

            def w1t_refresh():
                C.wait(te, "VE")
                for hc in range(4):
                    bk = pbt2[hc % 2]
                    gwait(te, bk)
                    for do in range(4):
                        C.inc(te.transpose(bk[:, do * 128:(do + 1) * 128],
                                           w1b[:, do, hc * 128:(hc + 1) * 128], identb[:, :]), "TE", 1)
                    C.wait(sc, "TE")
                    C.inc(sc.copy(out=w1t[:, hc, :], in_=bk[:, 0:512]), "SC", 1)
                    gset(bk, "SC")

            def head_init(h):
                C.wait(sy, "VE")
                C.wait(sy, "TE")
                C.wait(sy, "SC")
                C.inc(sy.dma_start(out=w0t_s[:], in_=w0t_h[h].rearrange("(c p) x -> p c x", p=128)), "Ds", 16)
                C.inc(sy.dma_start(out=w2t_s[:], in_=w2t_h[h].rearrange("(c p) x -> p c x", p=128)), "Ds", 16)
                C.inc(sy.dma_start(out=w1n_s[:], in_=w1n_h[h].rearrange("(c p) x -> p c x", p=128)), "Ds", 16)
                C.wait(ve, "Ds")
                C.inc(ve.tensor_copy(out=w0b[:].rearrange("p a b -> p (a b)"),
                                     in_=w0t_s[:].rearrange("p a b -> p (a b)")), "VE", 1)
                C.inc(ve.tensor_copy(out=w2b[:].rearrange("p a b -> p (a b)"),
                                     in_=w2t_s[:].rearrange("p a b -> p (a b)")), "VE", 1)
                C.inc(ve.tensor_copy(out=w1b[:].rearrange("p a b -> p (a b)"),
                                     in_=w1n_s[:].rearrange("p a b -> p (a b)")), "VE", 1)
                w1t_refresh()

            def chunk(h, gc, oc, qgc):
                gt0 = gc * 8
                C.wait(sc, "Do")
                C.wait(sy, "TE")
                C.wait(sy, "VE")
                C.wait(sy, "SC")
                C.inc(sy.dma_start(out=kF[:], in_=fkF_r[:, h * 4:(h + 1) * 4,
                                   gc * CH:(gc + 1) * CH]), "Ds", 16)
                C.inc(sy.dma_start(out=vF[:], in_=fvF_r[:, h * 4:(h + 1) * 4,
                                   gc * CH:(gc + 1) * CH]), "Ds", 16)
                C.inc(sy.dma_start(out=kT[:], in_=s_fkT[gc * CH:(gc + 1) * CH, h * FD:(h + 1) * FD]
                                   .rearrange("(t p) d -> p t d", p=128)), "Ds", 16)
                C.inc(sy.dma_start(out=vT[:], in_=s_fvT[gc * CH:(gc + 1) * CH, h * FD:(h + 1) * FD]
                                   .rearrange("(t p) d -> p t d", p=128)), "Ds", 16)
                if oc is not None:
                    C.inc(sy.dma_start(out=qF[:], in_=fq_r[:, h * 4:(h + 1) * 4,
                                       oc * CH:(oc + 1) * CH]), "Ds", 16)
                    C.inc(sy.dma_start(out=at[:], in_=s_attn[oc * CH:(oc + 1) * CH,
                                       h * FD:(h + 1) * FD]
                                       .rearrange("(t p) d -> p t d", p=128)), "Ds", 16)
                C.wait(te, "Ds")
                for tt in range(8):
                    l0 = lr_all[:, gt0 + tt, h:h + 1]
                    l1 = lr_all[:, gt0 + tt, 4 + h:5 + h]
                    l2 = lr_all[:, gt0 + tt, 8 + h:9 + h]
                    rk_col = rk_all[:, gt0 + tt, h:h + 1]
                    for bi, wmat in ((0, w0b), (1, w2b), (2, w1b)):
                        src = kF if bi < 2 else vF
                        gwait(te, pb[bi])
                        for dc in range(4):
                            C.inc(te.matmul(pb[bi][:, :], src[:, dc, tt * 128:(tt + 1) * 128],
                                            wmat[:, dc, :], start=(dc == 0), stop=(dc == 3)), "TE", 1)
                    C.wait(ve, "TE")
                    C.wait(ve, "VE")
                    C.inc(ve.tensor_scalar_mul(out=g32[:], in0=pb[0][:, :], scalar1=rk_col), "VE", 1)
                    gset(pb[0], "VE")
                    C.inc(ve.tensor_scalar_mul(out=h32[:], in0=pb[1][:, :], scalar1=rk_col), "VE", 1)
                    gset(pb[1], "VE")
                    C.inc(ve.tensor_copy(out=d32[:], in_=pb[2][:, :]), "VE", 1)
                    gset(pb[2], "VE")
                    C.wait(sc, "VE")
                    C.inc(sc.activation(out=sg[:], in_=g32[:], func=AF.Silu), "SC", 1)
                    C.inc(sc.activation(out=sig[:], in_=g32[:], func=AF.Sigmoid), "SC", 1)
                    C.wait(ve, "SC")
                    C.inc(ve.tensor_mul(out=hid[:, tt, :], in0=sg[:], in1=h32[:]), "VE", 1)
                    C.inc(ve.scalar_tensor_tensor(out=dhl2[:, tt, :], in0=d32[:], scalar=l2,
                                                  in1=sg[:], op0=ALU.mult, op1=ALU.mult), "VE", 1)
                    C.inc(ve.tensor_scalar(out=tf[:], in0=sig[:], scalar1=-1.0, scalar2=1.0,
                                           op0=ALU.mult, op1=ALU.add), "VE", 1)
                    C.wait(ve, "VE")
                    C.inc(ve.tensor_mul(out=t2[:], in0=g32[:], in1=tf[:]), "VE", 1)
                    C.wait(ve, "VE")
                    C.inc(ve.scalar_tensor_tensor(out=tf[:], in0=t2[:], scalar=1.0,
                                                  in1=sig[:], op0=ALU.add, op1=ALU.mult), "VE", 1)
                    C.inc(ve.tensor_mul(out=dgp[:], in0=d32[:], in1=h32[:]), "VE", 1)
                    C.wait(ve, "VE")
                    C.inc(ve.scalar_tensor_tensor(out=dgl0[:, tt, :], in0=dgp[:], scalar=l0,
                                                  in1=tf[:], op0=ALU.mult, op1=ALU.mult), "VE", 1)
                    C.inc(ve.tensor_scalar_mul(out=fvl1[:, tt, :], in0=vT[:, tt, :],
                                               scalar1=l1), "VE", 1)
                for wst, wbf, lh, rh in ((w0t_s, w0b, kT, dgl0), (w2t_s, w2b, kT, dhl2),
                                         (w1n_s, w1b, fvl1, hid)):
                    for dc in range(4):
                        gwait(te, pb[3])
                        C.wait(te, "VE")
                        for tt in range(8):
                            C.inc(te.matmul(pb[3][:, :], lh[:, tt, dc * 128:(dc + 1) * 128],
                                            rh[:, tt, :], start=(tt == 0), stop=(tt == 7)), "TE", 1)
                        C.wait(ve, "TE")
                        C.inc(ve.tensor_add(out=wst[:, dc, :], in0=wst[:, dc, :],
                                            in1=pb[3][:, :]), "VE", 1)
                        gset(pb[3], "VE")
                        C.wait(ve, "VE")
                        C.inc(ve.tensor_copy(out=wbf[:, dc, :], in_=wst[:, dc, :]), "VE", 1)
                w1t_refresh()
                if oc is None:
                    return
                for tt in range(8):
                    rq_col = rq_all[:, qgc * 8 + tt, h:h + 1]
                    gwait(te, pb[0])
                    C.wait(te, "SC")
                    for dc in range(4):
                        C.inc(te.matmul(pb[0][:, :], qF[:, dc, tt * 128:(tt + 1) * 128],
                                        w0b[:, dc, :], start=(dc == 0), stop=(dc == 3)), "TE", 1)
                    gwait(te, pb[1])
                    for dc in range(4):
                        C.inc(te.matmul(pb[1][:, :], qF[:, dc, tt * 128:(tt + 1) * 128],
                                        w2b[:, dc, :], start=(dc == 0), stop=(dc == 3)), "TE", 1)
                    C.wait(sc, "TE")
                    C.inc(sc.activation(out=sg[:], in_=pb[0][:, :], func=AF.Silu,
                                        scale=rq_col), "SC", 1)
                    gset(pb[0], "SC")
                    C.wait(ve, "SC")
                    C.inc(ve.scalar_tensor_tensor(out=hq_bf[:], in0=pb[1][:, :], scalar=rq_col,
                                                  in1=sg[:], op0=ALU.mult, op1=ALU.mult), "VE", 1)
                    gset(pb[1], "VE")
                    C.wait(te, "VE")
                    bk = pbt2[tt % 2]
                    gwait(te, bk)
                    for hc in range(4):
                        C.inc(te.transpose(bk[:, hc * 128:(hc + 1) * 128],
                                           hq_bf[:, hc * 128:(hc + 1) * 128], identb[:, :]), "TE", 1)
                    C.wait(sc, "TE")
                    for hc in range(4):
                        C.inc(sc.copy(out=hqT[:, hc, tt * 128:(tt + 1) * 128],
                                      in_=bk[:, hc * 128:(hc + 1) * 128]), "SC", 1)
                    gset(bk, "SC")
                for tt in range(8):
                    gwait(te, pb[2])
                    C.wait(te, "SC")
                    for hc in range(4):
                        C.inc(te.matmul(pb[2][:, :], hqT[:, hc, tt * 128:(tt + 1) * 128],
                                        w1t[:, hc, :], start=(hc == 0), stop=(hc == 3)), "TE", 1)
                    C.wait(sc, "TE")
                    C.inc(sc.activation(out=oisq[:], in_=pb[2][:, :], func=AF.Square,
                                        accum_out=ocol[:, 0:1]), "SC", 1)
                    C.wait(ve, "SC")
                    C.inc(ve.tensor_scalar(out=ocol[:, 1:2], in0=ocol[:, 0:1], scalar1=1.0 / FD,
                                           scalar2=1e-6, op0=ALU.mult, op1=ALU.add), "VE", 1)
                    C.wait(ve, "VE")
                    C.inc(ve.reciprocal(out=ocol[:, 1:2], in_=ocol[:, 1:2]), "VE", 1)
                    C.wait(sc, "VE")
                    C.inc(sc.sqrt(out=ocol[:, 2:3], in_=ocol[:, 1:2]), "SC", 1)
                    C.wait(sc, "SC")
                    C.inc(sc.activation(out=oi32[:], in_=pb[2][:, :], func=AF.Identity,
                                        scale=ocol[:, 2:3]), "SC", 1)
                    gset(pb[2], "SC")
                    C.wait(ve, "SC")
                    C.inc(ve.tensor_mul(out=oi32[:], in0=oi32[:], in1=ttwb[:, :]), "VE", 1)
                    C.wait(ve, "VE")
                    C.inc(ve.tensor_add(out=mixT[:], in0=oi32[:], in1=at[:, tt, :]), "VE", 1)
                    C.wait(te, "VE")
                    bk = pbt2[tt % 2]
                    gwait(te, bk)
                    for hc in range(4):
                        C.inc(te.transpose(bk[:, hc * 128:(hc + 1) * 128],
                                           mixT[:, hc * 128:(hc + 1) * 128], identb[:, :]), "TE", 1)
                    C.wait(sc, "TE")
                    for hc in range(4):
                        C.inc(sc.copy(out=mixF[:, hc, tt * 128:(tt + 1) * 128],
                                      in_=bk[:, hc * 128:(hc + 1) * 128]), "SC", 1)
                    gset(bk, "SC")
                C.wait(sy, "SC")
                C.inc(sy.dma_start(out=mixF_r[:, h * 4:(h + 1) * 4, oc * CH:(oc + 1) * CH],
                                   in_=mixF[:]), "Do", 16)

            def path(par):
                C.suffix = "5"
                for h in range(TNH):
                    head_init(h)
                    if par == 0:
                        chunk(h, 0, 0, 0)
                        chunk(h, 1, 1, 1)
                    else:
                        chunk(h, 0, None, None)
                        chunk(h, 1, None, None)
                        chunk(h, 2, 0, 2)
                        chunk(h, 3, 1, 3)
                for eng in C.engs:
                    for nm in list(C.cnt):
                        if nm.endswith("5"):
                            C._wait_raw(eng, nm, C.cnt[nm])
                    C.inc(eng.nop(), "PH", 1)
                C.suffix = ""

            pre_cnt = dict(C.cnt)
            prew = dict(C.waited)
            preg = dict(bank_guard)
            pre_ph = C.cnt.get("PH", 0)
            with nc.If_cmp(par_regs, 0, "IS_EQ"):
                path(0)
            C.cnt = dict(pre_cnt)
            C.cnt["PH"] = pre_ph + 5
            C.waited = dict(prew)
            bank_guard.clear()
            bank_guard.update(preg)
            with nc.Else():
                path(1)
            for nm in list(C.cnt):
                if nm.endswith("5"):
                    del C.cnt[nm]
            C.cnt["PH"] = pre_ph + 5
            tgt = C.cnt["PH"]
            for eng in C.engs:
                eng.wait_ge(C.sems["PH"], tgt)
            C.waited = {(e.engine, nm): C.cnt[nm] for e in C.engs for nm in C.cnt}
            bank_guard.clear()
            p5.close()

        def p6_wo():
            p6 = contextlib.ExitStack()
            wo_sb = sb("p6_wo", [128, DK, D], BF16, p6)
            mF = sb("p6_mF", [128, DK, TB], BF16, p6)
            ymix = sb("p6_ymix", [128, NTT, D], F32, p6)
            xb = sb("p6_xb", [128, NTT, D], F32, p6)
            lnscr = sb("p6_lnscr", [128, D], F32, p6)
            xmT = sb("p6_xmT", [128, NTT, D], BF16, p6)
            xmF = sb("p6_xmF", [128, DK, TB], BF16, p6)
            sm = sb("p6_sm", [128, NTT], F32, p6)
            sq = sb("p6_sq", [128, NTT], F32, p6)
            nmr = sb("p6_nmr", [128, NTT], F32, p6)
            sqs = sb("p6_sqs", [128, D], F32, p6)

            C.inc(gp.dma_start(out=wo_sb[:], in_=wo_h.rearrange("(c p) m -> p c m", p=128)), "Dw", 16)

            def ln_stats(buf):
                for tt in range(NTT):
                    xin = buf[:, tt, :]
                    C.wait(ve, "VE")
                    C.inc(ve.reduce_sum(out=sm[:, tt:tt + 1], in_=xin, axis=AX.X), "VE", 1)
                    C.inc(ve.scalar_tensor_tensor(out=sqs[:], in0=xin, scalar=1.0, in1=xin,
                                                  op0=ALU.mult, op1=ALU.mult,
                                                  accum_out=sq[:, tt:tt + 1]), "VE", 1)
                C.inc(ve.tensor_scalar_mul(out=sm[:], in0=sm[:], scalar1=1.0 / D), "VE", 1)
                C.inc(ve.tensor_scalar_mul(out=sq[:], in0=sq[:], scalar1=1.0 / D), "VE", 1)
                C.wait(ve, "VE")
                C.inc(ve.scalar_tensor_tensor(out=nmr[:], in0=sm[:], scalar=-1.0, in1=sm[:],
                                              op0=ALU.mult, op1=ALU.mult), "VE", 1)
                C.inc(ve.tensor_add(out=sq[:], in0=sq[:], in1=nmr[:]), "VE", 1)
                C.inc(ve.tensor_scalar_add(out=sq[:], in0=sq[:], scalar1=1e-5), "VE", 1)
                C.wait(ve, "VE")
                C.inc(ve.reciprocal(out=sq[:], in_=sq[:]), "VE", 1)
                C.wait(sc, "VE")
                C.inc(sc.sqrt(out=sq[:], in_=sq[:]), "SC", 1)
                C.wait(ve, "SC")
                C.inc(ve.scalar_tensor_tensor(out=nmr[:], in0=sm[:], scalar=-1.0, in1=sq[:],
                                              op0=ALU.mult, op1=ALU.mult), "VE", 1)
                C.wait(sc, "VE")

            for tb in range(4):
                C.wait(sy, "TE")
                C.wait(ve, "Do")
                C.wait(sc, "Do")
                C.wait(sy, "Do")
                C.wait(sy, "VE")
                C.wait(sy, "SC")
                C.inc(sy.dma_start(out=mF[:], in_=mixF_r[:, :, tb * TB:(tb + 1) * TB]), "Ds", 16)
                branch_pair(
                    lambda: C.inc(sy.dma_start(out=xb[:], in_=x_h[tb * TB:(tb + 1) * TB, :]
                                  .rearrange("(t p) d -> p t d", p=128)), "Dx", 16),
                    lambda: C.inc(sy.dma_start(out=xb[:], in_=x_h[2048 + tb * TB:2048 + (tb + 1) * TB, :]
                                  .rearrange("(t p) d -> p t d", p=128)), "Dx", 16))
                C.wait(te, "Ds")
                C.wait(te, "Dw")
                for tt in range(NTT):
                    for nd in range(4):
                        bank = pb[(tt * 4 + nd) % 2]
                        gwait(te, bank)
                        for dc in range(DK):
                            C.inc(te.matmul(bank[:, :], mF[:, dc, tt * 128:(tt + 1) * 128],
                                            wo_sb[:, dc, nd * 512:(nd + 1) * 512],
                                            start=(dc == 0), stop=(dc == DK - 1)), "TE", 1)
                        C.wait(ve, "TE")
                        C.inc(ve.tensor_copy(out=ymix[:, tt, nd * 512:(nd + 1) * 512],
                                             in_=bank[:, :]), "VE", 1)
                        gset(bank, "VE")
                C.wait(ve, "Dx")
                ln_stats(ymix)
                for tt in range(NTT):
                    C.wait(sc, "VE")
                    C.inc(sc.activation(out=lnscr[:], in_=ymix[:, tt, :], func=AF.Identity,
                                        bias=nmr[:, tt:tt + 1], scale=sq[:, tt:tt + 1]), "SC", 1)
                    C.wait(ve, "SC")
                    C.inc(ve.tensor_add(out=xb[:, tt, :], in0=xb[:, tt, :], in1=lnscr[:]), "VE", 1)
                C.wait(sy, "VE")
                C.inc(sy.dma_start(out=s_xmid[tb * TB:(tb + 1) * TB, :]
                                   .rearrange("(t p) d -> p t d", p=128), in_=xb[:]), "Do", 16)
                ln_stats(xb)
                for tt in range(NTT):
                    C.inc(sc.activation(out=xmT[:, tt, :], in_=xb[:, tt, :], func=AF.Identity,
                                        bias=nmr[:, tt:tt + 1], scale=sq[:, tt:tt + 1]), "SC", 1)
                C.wait(te, "SC")
                for dc in range(DK):
                    bk = pbt2[dc % 2]
                    gwait(te, bk)
                    for tt in range(NTT):
                        C.inc(te.transpose(bk[:, tt * 128:(tt + 1) * 128],
                                           xmT[:, tt, dc * 128:(dc + 1) * 128], identb[:, :]), "TE", 1)
                    C.wait(sc, "TE")
                    C.inc(sc.copy(out=xmF[:, dc, :], in_=bk[:, 0:512]), "SC", 1)
                    gset(bk, "SC")
                C.wait(sy, "SC")
                C.inc(sy.dma_start(out=xmidF_r[:, :, tb * TB:(tb + 1) * TB], in_=xmF[:]), "Do", 16)
            C.barrier()
            p6.close()

        def p7_ffn():
            for nb in range(4):
                p7 = contextlib.ExitStack()
                g_sb = sb(f"p7_g{nb}", [128, HT, TB], BF16, p7)
                ph = contextlib.ExitStack()
                xmF = sb(f"p7_xmF{nb}", [128, DK, TB], BF16, ph)
                sg_scr = sb(f"p7_sg{nb}", [128, TB], F32, ph)
                wA = sb(f"p7_wA{nb}", [128, DK, 1024], BF16, ph)
                wB = sb(f"p7_wB{nb}", [128, DK, 1024], BF16, ph)
                C.wait(sy, "TE")
                C.wait(sy, "Do")
                C.wait(sy, "VE")
                C.inc(sy.dma_start(out=xmF[:], in_=xmidF_r[:, :, nb * TB:(nb + 1) * TB]), "Ds", 16)
                for grp in range(6):
                    c0 = grp * 1024
                    cw = min(1024, HID - c0)
                    C.wait(sy, "TE")
                    C.inc(sy.dma_start(out=wA[:, :, 0:cw], in_=s_w1q[:, c0:c0 + cw]
                                       .rearrange("(c p) m -> p c m", p=128)), "Ds", 16)
                    C.inc(sy.dma_start(out=wB[:, :, 0:cw], in_=s_w3q[:, c0:c0 + cw]
                                       .rearrange("(c p) m -> p c m", p=128)), "Ds", 16)
                    C.wait(te, "Ds")
                    ntl = (cw + 127) // 128
                    for mi in range(ntl):
                        mt = grp * 8 + mi
                        rows = min(128, HID - mt * 128)
                        gwait(te, pb[0])
                        for dc in range(DK):
                            C.inc(te.matmul(pb[0][0:rows, :], wA[:, dc, mi * 128:mi * 128 + rows],
                                            xmF[:, dc, :], start=(dc == 0), stop=(dc == DK - 1)), "TE", 1)
                        gwait(te, pb[1])
                        for dc in range(DK):
                            C.inc(te.matmul(pb[1][0:rows, :], wB[:, dc, mi * 128:mi * 128 + rows],
                                            xmF[:, dc, :], start=(dc == 0), stop=(dc == DK - 1)), "TE", 1)
                        C.wait(sc, "TE")
                        C.wait(sc, "VE")
                        C.inc(sc.activation(out=sg_scr[0:rows, :], in_=pb[0][0:rows, :],
                                            func=AF.Silu, scale=s1col[0:rows, 0:1]), "SC", 1)
                        gset(pb[0], "SC")
                        C.wait(ve, "SC")
                        C.inc(ve.tensor_mul(out=g_sb[0:rows, mt, :], in0=sg_scr[0:rows, :],
                                            in1=pb[1][0:rows, :]), "VE", 1)
                        gset(pb[1], "VE")
                        if P7OUT == "h1dump" and nb == 0 and mt == 0:
                            C.wait(sy, "VE")
                            C.inc(sy.dma_start(out=y_h[384:512, 0:512], in_=sg_scr[:, :]), "Do", 16)
                            C.inc(gp.dma_start(out=y_h[512:640, 0:512], in_=g_sb[:, 0, :]), "Do", 16)
                            C.wait(ve, "Do")
                            C.inc(ve.tensor_copy(out=sg_scr[:, :], in_=pb[1][:, :]), "VE", 1)
                            C.wait(sy, "VE")
                            C.inc(sy.dma_start(out=y_h[640:768, 0:512], in_=sg_scr[:, :]), "Do", 16)
                            C.wait(sy, "Do")
                            C.wait(gp, "Do")
                            ph.close()
                            p7.close()
                            C.barrier()
                            return
                if P7OUT == "op42a" and nb == 0:
                    C.wait(gp, "VE")
                    for i4, cc in enumerate([39, 40, 41, 42]):
                        C.inc(gp.dma_start(out=y_h[i4 * 128:(i4 + 1) * 128, 0:512],
                                           in_=g_sb[:, cc, :]), "Do", 16)
                    C.wait(gp, "Do")
                    ph.close()
                    p7.close()
                    C.barrier()
                    return
                if P7OUT == "gdump2" and nb == 0:
                    C.wait(gp, "VE")
                    for q4 in range(4):
                        C.inc(gp.dma_start(out=y_h[q4 * 128:(q4 + 1) * 128, 0:512],
                                           in_=g_sb[:, GDTILES[q4], :]), "Do", 16)
                    C.wait(gp, "Do")
                    ph.close()
                    p7.close()
                    C.barrier()
                    return
                ph.close()
                py = contextlib.ExitStack()
                w2h = sb(f"p7_w2h{nb}", [128, HT, 512], BF16, py)
                y32 = sb(f"p7_y32{nb}", [128, NTT, D], F32, py)
                xb = sb(f"p7_xb{nb}", [128, NTT, D], F32, py)
                sm = sb(f"p7_sm{nb}", [128, NTT], F32, py)
                sq = sb(f"p7_sq{nb}", [128, NTT], F32, py)
                nmr = sb(f"p7_nmr{nb}", [128, NTT], F32, py)
                sqs = sb(f"p7_sqs{nb}", [128, D], F32, py)
                lnscr = sqs
                if P7OUT == "op42c" and nb == 0:
                    C.wait(gp, "VE")
                    for i4, cc in enumerate([39, 40, 41, 42]):
                        C.inc(gp.dma_start(out=y_h[i4 * 128:(i4 + 1) * 128, 0:512],
                                           in_=g_sb[:, cc, :]), "Do", 16)
                    C.wait(gp, "Do")
                    py.close()
                    p7.close()
                    C.barrier()
                    return
                C.wait(sy, "Do")
                C.wait(sy, "TE")
                for t4 in range(NTT):
                    C.inc(sy.dma_start(out=xb[:, t4, :],
                                       in_=s_xmid[nb * TB + t4 * 128:nb * TB + (t4 + 1) * 128, :]),
                          "Dx", 16)
                for ndq in range(4):
                    C.wait(ve, "Do")
                    if P7OUT == "op41" and nb == 0 and ndq == 0:
                        C.wait(gp, "VE")
                        for i4, cc in enumerate([39, 40, 41, 42]):
                            C.inc(gp.dma_start(out=y_h[i4 * 128:(i4 + 1) * 128, 0:512],
                                               in_=g_sb[:, cc, :]), "Do", 16)
                        C.wait(sy, "Do")
                    C.wait(sy, "TE")
                    C.wait(sy, "VE")
                    C.inc(sy.dma_start(out=w2h[:, 0:22, :],
                                       in_=s_w2q[0:22 * 128, ndq * 512:(ndq + 1) * 512]
                                       .rearrange("(c p) m -> p c m", p=128)), "Ds", 16)
                    C.inc(sy.dma_start(out=w2h[:, 22:HT, :],
                                       in_=s_w2q[22 * 128:HT * 128, ndq * 512:(ndq + 1) * 512]
                                       .rearrange("(c p) m -> p c m", p=128)), "Ds", 16)
                    C.wait(te, "Ds")
                    C.wait(te, "VE")
                    if P7OUT == "op41" and nb == 0 and ndq == 0:
                        C.wait(gp, "Ds")
                        for i4, cc in enumerate([39, 40, 41, 42]):
                            C.inc(gp.dma_start(out=y_h[512 + i4 * 128:512 + (i4 + 1) * 128, 0:512],
                                               in_=g_sb[:, cc, :]), "Do", 16)
                        C.wait(gp, "Do")
                        py.close()
                        p7.close()
                        C.barrier()
                        return
                    if P7OUT == "op40" and nb == 0 and ndq == 0:
                        g2s = sb("p7_g2s", [128, 512], BF16, py)
                        w2s = sb("p7_w2s", [128, 512], BF16, py)
                        C.wait(gp, "Ds")
                        C.wait(gp, "VE")
                        C.inc(gp.dma_start(out=y_h[0:128, 0:512], in_=g_sb[:, 40, :]), "Do", 16)
                        C.inc(gp.dma_start(out=y_h[128:256, 0:512], in_=w2h[:, 40, :]), "Do", 16)
                        bank = pb[2]
                        gwait(te, bank)
                        C.inc(te.matmul(bank[:, :], g_sb[0:128, 40, 0:128],
                                        w2h[0:128, 40, :], start=True, stop=True), "TE", 1)
                        C.wait(ve, "TE")
                        C.inc(ve.tensor_copy(out=y32[:, 0, 0:512], in_=bank[:, :]), "VE", 1)
                        gset(bank, "VE")
                        C.wait(sy, "VE")
                        C.inc(sy.dma_start(out=y_h[256:384, 0:512], in_=y32[:, 0, 0:512]), "Do", 16)
                        C.wait(ve, "Ds")
                        C.inc(ve.tensor_copy(out=g2s[:, :], in_=g_sb[:, 40, :]), "VE", 1)
                        C.inc(ve.tensor_copy(out=w2s[:, :], in_=w2h[:, 40, :]), "VE", 1)
                        C.wait(te, "VE")
                        bank2 = pb[3]
                        gwait(te, bank2)
                        C.inc(te.matmul(bank2[:, :], g2s[:, 0:128], w2s[:, :],
                                        start=True, stop=True), "TE", 1)
                        C.wait(ve, "TE")
                        C.inc(ve.tensor_copy(out=y32[:, 1, 0:512], in_=bank2[:, :]), "VE", 1)
                        gset(bank2, "VE")
                        C.wait(sy, "VE")
                        C.inc(sy.dma_start(out=y_h[384:512, 0:512], in_=y32[:, 1, 0:512]), "Do", 16)
                        C.wait(sy, "Do")
                        C.wait(gp, "Do")
                        py.close()
                        p7.close()
                        C.barrier()
                        return
                    if P7OUT == "w2dump" and nb == 0 and ndq == 0:
                        C.wait(gp, "Ds")
                        for i4, cc in enumerate([39, 40, 41, 42]):
                            C.inc(gp.dma_start(out=y_h[i4 * 128:(i4 + 1) * 128, 0:512],
                                               in_=w2h[:, cc, :]), "Do", 16)
                        C.wait(gp, "Do")
                        py.close()
                        p7.close()
                        C.barrier()
                        return
                    for tt in range(NTT):
                        bank = pb[2 + tt % 2]
                        gwait(te, bank)
                        hcs = list(range(min(HT, YHC))) if not isinstance(YHC, (list, tuple)) else list(YHC)
                        for ii, hc in enumerate(hcs):
                            rows = min(128, HID - hc * 128)
                            C.inc(te.matmul(bank[:, :], g_sb[0:rows, hc, tt * 128:(tt + 1) * 128],
                                            w2h[0:rows, hc, :],
                                            start=(ii == 0), stop=(ii == len(hcs) - 1)), "TE", 1)
                        C.wait(ve, "TE")
                        C.inc(ve.tensor_copy(out=y32[:, tt, ndq * 512:(ndq + 1) * 512],
                                             in_=bank[:, :]), "VE", 1)
                        gset(bank, "VE")
                        if P7OUT == "ydump" and nb == 0 and ndq == 0 and tt == 0:
                            C.wait(sy, "VE")
                            C.inc(sy.dma_start(out=y_h[0:128, 0:512], in_=y32[:, 0, 0:512]), "Do", 16)
                            C.wait(sy, "Do")
                            py.close()
                            p7.close()
                            C.barrier()
                            return
                if P7OUT == "y32raw":
                    C.wait(sy, "VE")
                    C.inc(sy.dma_start(out=y_h[nb * TB:(nb + 1) * TB, :]
                                       .rearrange("(t p) d -> p t d", p=128), in_=y32[:]), "Do", 16)
                    py.close()
                    p7.close()
                    continue
                if P7OUT == "gdump":
                    C.wait(sy, "VE")
                    C.wait(sy, "SC")
                    for q4 in range(4):
                        C.inc(gp.dma_start(out=y_h[q4 * 512:q4 * 512 + 128, nb * 512:(nb + 1) * 512],
                                           in_=g_sb[:, nb * 4 + q4, :]), "Do", 16)
                    C.inc(sy.dma_start(out=y_h[1900:1900 + 128, nb * 4:nb * 4 + 1], in_=s1col[:, :], allow_slow_non_contiguous=True), "Do", 16)
                    if nb == 0:
                        C.inc(sy.dma_start(out=wA[:, :, 0:1024], in_=s_w1q[:, 0:1024]
                                           .rearrange("(c p) m -> p c m", p=128)), "Ds", 16)
                        C.wait(gp, "Ds")
                        C.inc(gp.dma_start(out=y_h[512:512 + 128, 512:1024], in_=wA[:, 0, 0:512]), "Do", 16)
                        C.inc(gp.dma_start(out=y_h[640:640 + 128, 512:1024], in_=xmF[:, 0, 0:512]), "Do", 16)
                    py.close()
                    p7.close()
                    continue
                C.wait(ve, "Dx")
                for tt in range(NTT):
                    C.inc(ve.tensor_scalar_mul(out=y32[:, tt, :], in0=y32[:, tt, :],
                                               scalar1=s23col[:, 0:1]), "VE", 1)
                for tt in range(NTT):
                    xin = y32[:, tt, :]
                    C.wait(ve, "VE")
                    C.inc(ve.reduce_sum(out=sm[:, tt:tt + 1], in_=xin, axis=AX.X), "VE", 1)
                    C.inc(ve.scalar_tensor_tensor(out=sqs[:], in0=xin, scalar=1.0, in1=xin,
                                                  op0=ALU.mult, op1=ALU.mult,
                                                  accum_out=sq[:, tt:tt + 1]), "VE", 1)
                C.inc(ve.tensor_scalar_mul(out=sm[:], in0=sm[:], scalar1=1.0 / D), "VE", 1)
                C.inc(ve.tensor_scalar_mul(out=sq[:], in0=sq[:], scalar1=1.0 / D), "VE", 1)
                C.wait(ve, "VE")
                C.inc(ve.scalar_tensor_tensor(out=nmr[:], in0=sm[:], scalar=-1.0, in1=sm[:],
                                              op0=ALU.mult, op1=ALU.mult), "VE", 1)
                C.inc(ve.tensor_add(out=sq[:], in0=sq[:], in1=nmr[:]), "VE", 1)
                C.inc(ve.tensor_scalar_add(out=sq[:], in0=sq[:], scalar1=1e-5), "VE", 1)
                C.wait(ve, "VE")
                C.inc(ve.reciprocal(out=sq[:], in_=sq[:]), "VE", 1)
                C.wait(sc, "VE")
                C.inc(sc.sqrt(out=sq[:], in_=sq[:]), "SC", 1)
                C.wait(ve, "SC")
                C.inc(ve.scalar_tensor_tensor(out=nmr[:], in0=sm[:], scalar=-1.0, in1=sq[:],
                                              op0=ALU.mult, op1=ALU.mult), "VE", 1)
                C.wait(sc, "VE")
                for tt in range(NTT):
                    C.wait(sc, "VE")
                    C.inc(sc.activation(out=lnscr[:], in_=y32[:, tt, :], func=AF.Identity,
                                        bias=nmr[:, tt:tt + 1], scale=sq[:, tt:tt + 1]), "SC", 1)
                    C.wait(ve, "SC")
                    C.inc(ve.tensor_add(out=y32[:, tt, :], in0=xb[:, tt, :], in1=lnscr[:]), "VE", 1)
                if P7OUT == "y32pre":
                    pass
                C.wait(sy, "VE")
                C.inc(sy.dma_start(out=y_h[nb * TB:(nb + 1) * TB, :]
                                   .rearrange("(t p) d -> p t d", p=128), in_=y32[:]), "Do", 16)
                py.close()
                p7.close()
            C.barrier()
        def dump_and_finish():
            for name, out_ap in dbg_out.items():
                if name in ("s_rq", "s_rk", "s_lr"):
                    continue
                src = scratch[name]
                C.inc(sy.dma_start(out=out_ap[:, :] if len(out_ap.shape) == 2 else out_ap[:],
                                   in_=src[:, :] if len(src.shape) == 2 else src[:]), "Do", 16)
            if "s_rq" in dbg_out or "s_lr" in dbg_out:
                C.wait(sy, "VE")
            if "s_rq" in dbg_out:
                C.inc(sy.dma_start(out=s_rq.rearrange("(c p) k -> p c k", p=128),
                                   in_=rq_all[:]), "Do", 16)
                C.inc(sy.dma_start(out=s_rk.rearrange("(c p) k -> p c k", p=128),
                                   in_=rk_all[:]), "Do", 16)
            if "s_lr" in dbg_out:
                C.inc(sy.dma_start(out=s_lr.rearrange("(c p) k -> p c k", p=128),
                                   in_=lr_all[:]), "Do", 16)
            C.wait(sy, "Do")
            if "s_rq" in dbg_out:
                C.inc(sy.dma_start(out=dbg_out["s_rq"][:, :], in_=s_rq[:, :]), "Do", 16)
                C.inc(sy.dma_start(out=dbg_out["s_rk"][:, :], in_=s_rk[:, :]), "Do", 16)
            if "s_lr" in dbg_out:
                C.inc(sy.dma_start(out=dbg_out["s_lr"][:, :], in_=s_lr[:, :]), "Do", 16)
            C.wait(sy, "Do")

        @block.sync
        def _(_sync):
            p0_setup()
            if PHASES >= 1:
                p1_layernorm()
            if PHASES >= 2:
                p2_all(parts=P2PARTS)
            if PHASES >= 3:
                p3_quant()
            if PHASES >= 4:
                p4_attention()
            if PHASES >= 5:
                p5_ttt()
            if PHASES >= 6:
                p6_wo()
            if PHASES >= 7:
                p7_ffn()
            dump_and_finish()

    stack.close()
    return nc


_CACHE = {}


def _host_inputs(inputs):
    x = np.ascontiguousarray(np.asarray(inputs["L0_attn_hidden_states"]), dtype=np.float32)
    wqkv = np.asarray(inputs["L0_attn_Wqkv"])
    lr_w = np.asarray(inputs["L0_attn_lr_w"])
    wqk = np.ascontiguousarray(
        np.concatenate([wqkv[0:D].T, wqkv[D:2 * D].T, lr_w.T], axis=1), dtype=np.float32)
    wv = np.ascontiguousarray(wqkv[2 * D:3 * D].T, dtype=np.float32)
    wo = np.ascontiguousarray(np.asarray(inputs["L0_attn_Wo"]).T, dtype=np.float32)
    w0t = np.ascontiguousarray(np.asarray(inputs["L0_attn_w0"]).transpose(0, 2, 1), dtype=np.float32)
    w2t = np.ascontiguousarray(np.asarray(inputs["L0_attn_w2"]).transpose(0, 2, 1), dtype=np.float32)
    w1n = np.ascontiguousarray(np.asarray(inputs["L0_attn_w1"]), dtype=np.float32)
    qs = np.asarray(inputs["L0_attn_qk_scale"])
    qo = np.asarray(inputs["L0_attn_qk_offset"])
    vecs = np.ascontiguousarray(np.stack(
        [np.asarray(inputs["L0_attn_q_norm_w"]), np.asarray(inputs["L0_attn_k_norm_w"]),
         qs[:, 0], qo[:, 0], qs[:, 1], qo[:, 1],
         np.zeros(D, np.float32), np.zeros(D, np.float32)], axis=1), dtype=np.float32)
    lrb = np.zeros((128, 1), np.float32)
    lrb[0:12, 0] = np.asarray(inputs["L0_attn_lr_b"])
    ttw = np.ascontiguousarray(np.asarray(inputs["L0_attn_ttt_norm_w"])[None, :], dtype=np.float32)
    wf1 = np.ascontiguousarray(np.asarray(inputs["L0_ffn_w1"]).T, dtype=np.float32)
    wf3 = np.ascontiguousarray(np.asarray(inputs["L0_ffn_w3"]).T, dtype=np.float32)
    wf2 = np.ascontiguousarray(np.asarray(inputs["L0_ffn_w2"]).T, dtype=np.float32)
    invf = 1.0 / (500000.0 ** (np.arange(0, HD, 2, dtype=np.float64) / HD))
    ang = np.arange(S, dtype=np.float64)[None, :] * invf[:, None]
    ropec = np.ascontiguousarray(np.cos(ang), dtype=np.float32)
    ropes = np.ascontiguousarray(np.sin(ang), dtype=np.float32)
    r = np.arange(128)[:, None]
    c = np.arange(128)[None, :]
    masks = np.ascontiguousarray(np.stack([np.where(c > r, 0.0, NEG),
                                           np.where(c <= r, 0.0, NEG)]), dtype=np.float32)
    ident = np.eye(128, dtype=np.float32)
    in_maps = []
    for core in range(8):
        bb = core // 2
        par = core % 2
        ninv = np.zeros((128, 16), np.float32)
        if par == 0:
            rr = np.arange(128, dtype=np.float32)
            for j in range(8):
                ninv[:, j] = 128.0 * (8 - j) - (rr + 1.0)
        meta = np.zeros((1, 8), np.int32)
        meta[0, 0] = par
        in_maps.append({
            "x": x[bb], "wqk": wqk, "wv": wv, "wo": wo,
            "w0t": w0t, "w2t": w2t, "w1n": w1n,
            "vecs": vecs, "lrb": lrb, "ttw": ttw,
            "wf1": wf1, "wf3": wf3, "wf2": wf2,
            "ropec": ropec, "ropes": ropes,
            "masks": masks, "ninv": ninv,
            "ident": ident, "meta": meta,
        })
    return in_maps


def kernel(**inputs):
    in_maps = _host_inputs(inputs)
    if "nc" not in _CACHE:
        _CACHE["nc"] = build_program()
    nc = _CACHE["nc"]
    res = run_bass_kernel_spmd(nc, in_maps, core_ids=list(range(8)))
    out = np.empty((4, S, D), np.float32)
    for c in range(8):
        out[c // 2, (c % 2) * TOWN:(c % 2 + 1) * TOWN, :] = res.results[c]["y"]
    return out
